# version 27
# baseline (speedup 1.0000x reference)
import contextlib

import numpy as np
import concourse.bass as bass
import concourse.mybir as mybir
from concourse.bass_utils import run_bass_kernel_spmd

F32 = mybir.dt.float32
BF16 = mybir.dt.bfloat16
I32 = mybir.dt.int32
ALU = mybir.AluOpType
ACT = mybir.ActivationFunctionType

P = 128
ND = 8
NE = 8
NH = 24
NT = 4
TL = 2048
TS = 512
D = 1024
EPS = 1e-5
LN127 = float(np.log(127.0))

ENGINES = ("sync", "gpsimd", "scalar", "vector", "tensor")


class Sched:
    def __init__(self):
        self.q = {e: [] for e in ENGINES}
        self.cnt = {}

    def emit(self, eng, fn):
        self.q[eng].append(fn)

    def inc(self, sem, n=1):
        self.cnt[sem] = self.cnt.get(sem, 0) + n
        return self.cnt[sem]

    def val(self, sem):
        return self.cnt.get(sem, 0)


def build_nc():
    nc = bass.Bass("TRN2", target_bir_lowering=False, debug=False)

    xt = nc.dram_tensor("xt", [P, ND * TL], F32, kind="ExternalInput").ap()
    wint = nc.dram_tensor("wint", [P, ND * D], F32, kind="ExternalInput").ap()
    wont = nc.dram_tensor("wont", [P, NE * D], F32, kind="ExternalInput").ap()
    wg = nc.dram_tensor("wg", [NH, P, ND * P], I32, kind="ExternalInput").ap()
    wu = nc.dram_tensor("wu", [NH, P, ND * P], I32, kind="ExternalInput").ap()
    wd = nc.dram_tensor("wd", [NH, P, ND * P], I32, kind="ExternalInput").ap()
    vpp = nc.dram_tensor("vpp", [P, 44], F32, kind="ExternalInput").ap()
    rows = nc.dram_tensor("rows", [5, D], F32, kind="ExternalInput").ap()
    selv = nc.dram_tensor("selv", [8, 1], F32, kind="ExternalInput").ap()
    yout = nc.dram_tensor("yout", [P, ND * TL], F32, kind="ExternalOutput").ap()

    wdbf = nc.dram_tensor("wdbf", [NH, P, ND * P], BF16)
    cc_in = nc.dram_tensor("cc_in", [P * NE], F32)
    cc_out = nc.dram_tensor("cc_out", [8, P * NE], F32)

    sch = Sched()

    es = contextlib.ExitStack()
    with es:
        BF1 = es.enter_context(nc.sbuf_tensor("BF1", [P, ND * TL], BF16))
        USC = es.enter_context(nc.sbuf_tensor("USC", [P, ND * TL], F32))
        HH = es.enter_context(nc.sbuf_tensor("HH", [P, NH * 1024], BF16))
        WIN = es.enter_context(nc.sbuf_tensor("WIN", [P, ND * D], BF16))
        BC = es.enter_context(nc.sbuf_tensor("BC", [P, TL], F32))
        SCNBUF = es.enter_context(nc.sbuf_tensor("SCNBUF", [P, TL], F32))
        RS1 = es.enter_context(nc.sbuf_tensor("RS1", [P, 1024], F32))
        RS2 = es.enter_context(nc.sbuf_tensor("RS2", [P, 1024], F32))
        RMU = es.enter_context(nc.sbuf_tensor("RMU", [P, 1024], F32))
        BMU = es.enter_context(nc.sbuf_tensor("BMU", [P, 1024], BF16))
        BSIG = es.enter_context(nc.sbuf_tensor("BSIG", [P, 1024], BF16))
        BINV = es.enter_context(nc.sbuf_tensor("BINV", [P, 1024], BF16))
        W1B = es.enter_context(nc.sbuf_tensor("W1B", [P, D], BF16))
        RBIN = es.enter_context(nc.sbuf_tensor("RBIN", [P, D], BF16))
        RBOUT = es.enter_context(nc.sbuf_tensor("RBOUT", [P, D], BF16))
        RGAM = es.enter_context(nc.sbuf_tensor("RGAM", [P, D], BF16))
        XSQ = es.enter_context(nc.sbuf_tensor("XSQ", [P, 2 * TS], BF16))
        ONES = es.enter_context(nc.sbuf_tensor("ONES", [P, TS], BF16))
        VPP = es.enter_context(nc.sbuf_tensor("VPP", [P, 44], F32))
        APP = es.enter_context(nc.sbuf_tensor("APP", [P, 8], F32))
        TM8 = es.enter_context(nc.sbuf_tensor("TM8", [P, 8], F32))
        DEC1 = es.enter_context(nc.sbuf_tensor("DEC1", [P, 1], F32))
        ZCOL = es.enter_context(nc.sbuf_tensor("ZCOL", [P, 1], F32))
        SEND = es.enter_context(nc.sbuf_tensor("SEND", [P, NE], F32))
        SST = es.enter_context(nc.sbuf_tensor("SST", [P, NE], F32))
        PT = es.enter_context(nc.sbuf_tensor("PT", [P, NE], F32))
        GBF = es.enter_context(nc.sbuf_tensor("GBF", [P, P * NE], BF16))
        SELB = es.enter_context(nc.sbuf_tensor("SELB", [8, 1], BF16))
        PS0 = es.enter_context(nc.psum_tensor("PS0", [P, TS], F32))
        PS1 = es.enter_context(nc.psum_tensor("PS1", [P, TS], F32))
        PS2 = es.enter_context(nc.psum_tensor("PS2", [P, TS], F32))
        PS3 = es.enter_context(nc.psum_tensor("PS3", [P, TS], F32))
        PS4 = es.enter_context(nc.psum_tensor("PS4", [P, TS], F32))
        PS5 = es.enter_context(nc.psum_tensor("PS5", [P, TS], F32))
        PS6 = es.enter_context(nc.psum_tensor("PS6", [P, TS], F32))
        PS7 = es.enter_context(nc.psum_tensor("PS7", [P, TS], F32))
        LDG = es.enter_context(nc.semaphore("LDG"))
        LDS = es.enter_context(nc.semaphore("LDS"))
        OUTS = es.enter_context(nc.semaphore("OUT"))
        TMM = es.enter_context(nc.semaphore("TMM"))
        VEV = es.enter_context(nc.semaphore("VEV"))
        SCL = es.enter_context(nc.semaphore("SCL"))
        CCS = es.enter_context(nc.semaphore("CCS"))
        GPS = es.enter_context(nc.semaphore("GPS"))
        XW0 = es.enter_context(nc.semaphore("XW0"))
        XW1 = es.enter_context(nc.semaphore("XW1"))
        SO0 = es.enter_context(nc.semaphore("SO0"))
        SO1 = es.enter_context(nc.semaphore("SO1"))
        WGS0 = es.enter_context(nc.semaphore("WGS0"))
        WGS1 = es.enter_context(nc.semaphore("WGS1"))
        WUS0 = es.enter_context(nc.semaphore("WUS0"))
        WUS1 = es.enter_context(nc.semaphore("WUS1"))
        WDS0 = es.enter_context(nc.semaphore("WDS0"))
        WDS1 = es.enter_context(nc.semaphore("WDS1"))
        WGS2 = es.enter_context(nc.semaphore("WGS2"))
        WUS2 = es.enter_context(nc.semaphore("WUS2"))
        WDS2 = es.enter_context(nc.semaphore("WDS2"))
        WDS3 = es.enter_context(nc.semaphore("WDS3"))
        XAC = es.enter_context(nc.semaphore("XAC"))
        WB0 = es.enter_context(nc.semaphore("WB0"))
        WB1 = es.enter_context(nc.semaphore("WB1"))
        WB2 = es.enter_context(nc.semaphore("WB2"))
        WB3 = es.enter_context(nc.semaphore("WB3"))
        WD4 = es.enter_context(nc.semaphore("WD4"))
        WD5 = es.enter_context(nc.semaphore("WD5"))
        WD6 = es.enter_context(nc.semaphore("WD6"))
        WD7 = es.enter_context(nc.semaphore("WD7"))
        LX0 = es.enter_context(nc.semaphore("LX0"))
        LX1 = es.enter_context(nc.semaphore("LX1"))
        LX2 = es.enter_context(nc.semaphore("LX2"))
        LX3 = es.enter_context(nc.semaphore("LX3"))
        block = es.enter_context(nc.Block())
        XRS = SCNBUF
        TMPV = BSIG.ap().bitcast(F32)
        SG = GBF
        PS = [PS0, PS1, PS2, PS3, PS4, PS5, PS6, PS7]
        XW = [XW0, XW1]
        SO = [SO0, SO1]
        WGS = [WGS0, WGS1, WGS2]
        WUS = [WUS0, WUS1, WUS2]
        WDS = [WDS0, WDS1, WDS2, WDS3]
        WB = [WB0, WB1, WB2, WB3]
        WDSX = [WDS0, WDS1, WDS2, WDS3, WD4, WD5, WD6, WD7]
        LX = [LX0, LX1, LX2, LX3]
        WINF = WIN.ap().bitcast(F32)

        def xbf(dch, tt):
            return BF1[:, dch * TL + tt * TS: dch * TL + (tt + 1) * TS]

        bf1 = xbf

        def bf1full(ch):
            return BF1[:, ch * TL:(ch + 1) * TL]

        def usc(ch, tt):
            return USC[:, ch * TL + tt * TS: ch * TL + (tt + 1) * TS]

        def uscfull(ch):
            return USC[:, ch * TL:(ch + 1) * TL]

        def win(dch, ech):
            return WIN[:, dch * D + ech * P: dch * D + (ech + 1) * P]

        TMPVB = TMPV.bitcast(BF16)
        XSQB = XSQ.ap()
        BCB = BC.ap().bitcast(BF16)

        def wgb(slot):
            if slot < 2:
                return WIN[:, slot * 1024:(slot + 1) * 1024]
            return W1B[:, :]

        def wdb(slot):
            if slot < 2:
                return WIN[:, 2048 + slot * 1024: 2048 + (slot + 1) * 1024]
            return BCB[:, (slot - 2) * 1024:(slot - 1) * 1024]

        SCNB = SCNBUF.ap().bitcast(BF16)

        def wdb2(slot):
            if slot < 2:
                return WIN[:, 2048 + slot * 1024: 2048 + (slot + 1) * 1024]
            if slot < 4:
                return BCB[:, (slot - 2) * 1024:(slot - 1) * 1024]
            return SCNB[:, (slot - 4) * 1024:(slot - 3) * 1024]

        def wub(slot):
            if slot < 2:
                return WIN[:, 4096 + slot * 1024: 4096 + (slot + 1) * 1024]
            return RBIN[:, :]

        def stg(slot):
            return WINF[:, 3072 + slot * TS: 3072 + (slot + 1) * TS]

        def row(R, tt):
            return R[32 * (tt // 2): 32 * (tt // 2) + 1, (tt % 2) * TS:(tt % 2) * TS + TS]

        def rowp(tt):
            return 32 * (tt // 2)

        def vcol(kind, ch):
            base = {"log_a": 0, "b_p": 8, "c_p": 16, "gamma": 24, "beta": 32}[kind]
            return VPP[:, base + ch: base + ch + 1]

        def ccol(name, p0=0, np_=P):
            i = {"eps": 40, "mhalf": 41, "phalf": 42, "mln127": 43}[name]
            return VPP[p0:p0 + np_, i:i + 1]

        def bc(tt):
            return BC[:, (tt % 4) * TS:(tt % 4 + 1) * TS]

        sch.emit("gpsimd", lambda g: g.dma_start(out=WIN[:, :], in_=wint[:, :]).then_inc(LDG, 16))
        LD_WIN = sch.inc("LDG", 16)
        for tt in range(NT):
            sch.emit("gpsimd", lambda g, t=tt: g.dma_start(
                out=BF1.ap().rearrange("p (c t) -> p c t", c=ND)[:, :, t * TS:(t + 1) * TS],
                in_=xt.rearrange("p (c t) -> p c t", c=ND)[:, :, t * TS:(t + 1) * TS],
            ).then_inc(LX[t], 16))
            sch.inc("LX%d" % tt, 16)
        sch.emit("gpsimd", lambda g: g.dma_start(out=RBIN[0:1, :], in_=rows[0:1, :]).then_inc(LDG, 16))
        sch.inc("LDG", 16)
        sch.emit("gpsimd", lambda g: g.dma_start(out=RBOUT[0:1, :], in_=rows[1:2, :]).then_inc(LDG, 16))
        sch.inc("LDG", 16)
        sch.emit("gpsimd", lambda g: g.dma_start(out=RGAM[0:2, :], in_=rows[2:4, :]).then_inc(LDG, 16))
        sch.inc("LDG", 16)
        sch.emit("gpsimd", lambda g: g.dma_start(out=BMU[1:2, :], in_=rows[4:5, :]).then_inc(LDG, 16))
        sch.inc("LDG", 16)
        sch.emit("gpsimd", lambda g: g.dma_start(out=BMU[33:34, :], in_=rows[4:5, :]).then_inc(LDG, 16))
        sch.inc("LDG", 16)
        sch.emit("gpsimd", lambda g: g.dma_start(out=SELB[:, :], in_=selv[:, :]).then_inc(LDG, 16))
        LD_SMALL = sch.inc("LDG", 16)

        sch.emit("sync", lambda s: s.dma_start(out=VPP[:, :], in_=vpp[:, :]).then_inc(LDS, 16))
        LD_VPP = sch.inc("LDS", 16)

        sch.emit("vector", lambda v: v.memset(ONES[:, :], 1.0))
        sch.emit("vector", lambda v: v.memset(ZCOL[:, :], 0.0))
        sch.emit("vector", lambda v: v.wait_ge(LDG, LD_SMALL))
        sch.emit("vector", lambda v: v.tensor_copy(RBIN[32:33, :], RBIN[0:1, :]))
        sch.emit("vector", lambda v: v.tensor_copy(RBOUT[32:33, :], RBOUT[0:1, :]))
        sch.emit("vector", lambda v: v.tensor_copy(RGAM[32:34, :], RGAM[0:2, :]))

        sch.emit("scalar", lambda s: s.wait_ge(LDS, LD_VPP))
        sch.emit("scalar", lambda s: s.activation(TM8[:, :], VPP[:, 0:8], ACT.Exp))
        sch.emit("scalar", lambda s: s.activation(TM8[:, :], TM8[:, :], ACT.Ln, bias=1.0))
        sch.emit("scalar", lambda s: s.activation(APP[:, :], TM8[:, :], ACT.Exp, scale=-1.0).then_inc(SCL, 1))
        S_DECAY = sch.inc("SCL")

        def stats_pass(src_tile_fn):
            for tt in range(NT):
                for ch in range(ND):
                    slot = XSQ[:, (ch % 2) * TS:(ch % 2 + 1) * TS]
                    if tt * ND + ch >= 2:
                        need = sch.val("TMM") - 1
                        sch.emit("vector", lambda v, need=need: v.wait_ge(TMM, need))
                    sch.emit("vector", lambda v, s=slot, c=ch, t=tt: v.tensor_tensor(
                        s, src_tile_fn(c, t), src_tile_fn(c, t), op=ALU.mult).then_inc(VEV, 1))
                    vney = sch.inc("VEV")
                    sch.emit("tensor", lambda t_, vney=vney: t_.wait_ge(VEV, vney))
                    sch.emit("tensor", lambda t_, c=ch, t=tt: t_.matmul(
                        PS6[0:1, :], ONES[:, 0:1], src_tile_fn(c, t),
                        start=(c == 0), stop=(c == ND - 1)))
                    sch.emit("tensor", lambda t_, s=slot, c=ch: t_.matmul(
                        PS7[0:1, :], ONES[:, 0:1], s,
                        start=(c == 0), stop=(c == ND - 1)).then_inc(TMM, 1))
                    sch.inc("TMM")
                t_stats = sch.val("TMM")
                sch.emit("vector", lambda v, t_stats=t_stats: v.wait_ge(TMM, t_stats))
                sch.emit("vector", lambda v, t=tt: v.tensor_copy(row(RS1, t), PS6[0:1, :]))
                sch.emit("vector", lambda v, t=tt: v.tensor_copy(row(RS2, t), PS7[0:1, :]).then_inc(VEV, 1))
                v_ev = sch.inc("VEV")
                sch.emit("tensor", lambda t_, v_ev=v_ev: t_.wait_ge(VEV, v_ev))

        def ln_rows(with_sigma, inv_bias):
            sch.emit("vector", lambda v: v.tensor_scalar(
                RMU[:, :], RS1[:, :], 1.0 / D, None, op0=ALU.mult))
            sch.emit("vector", lambda v: v.tensor_tensor(
                RS1[:, :], RMU[:, :], RMU[:, :], op=ALU.mult))
            sch.emit("vector", lambda v: v.tensor_scalar(
                RS2[:, :], RS2[:, :], 1.0 / D, None, op0=ALU.mult))
            sch.emit("vector", lambda v: v.tensor_tensor(
                RS2[:, :], RS2[:, :], RS1[:, :], op=ALU.subtract).then_inc(VEV, 1))
            v_rows = sch.inc("VEV")
            sch.emit("scalar", lambda s, v_rows=v_rows: s.wait_ge(VEV, v_rows))
            sch.emit("scalar", lambda s: s.activation(RS2[:, :], RS2[:, :], ACT.Ln, bias=ccol("eps")))
            if inv_bias == 0.0:
                sch.emit("scalar", lambda s: s.activation(
                    BINV[:, :], RS2[:, :], ACT.Exp, scale=ccol("mhalf")))
            else:
                sch.emit("scalar", lambda s: s.activation(
                    BINV[:, :], RS2[:, :], ACT.Exp, scale=ccol("mhalf"), bias=ccol("mln127")))
            if with_sigma:
                sch.emit("scalar", lambda s: s.activation(
                    BSIG[:, :], RS2[:, :], ACT.Exp, scale=ccol("phalf")))
            sch.emit("scalar", lambda s: s.activation(
                BMU[0:1, :], RMU[0:1, :], ACT.Copy))
            sch.emit("scalar", lambda s: s.activation(
                BMU[32:33, :], RMU[32:33, :], ACT.Copy).then_inc(SCL, 1))
            return sch.inc("SCL")

        def bcast_inv(rowsrc):
            for tt in range(NT):
                p = rowp(tt)
                sch.emit("tensor", lambda t_, p=p, t=tt, R=rowsrc: t_.matmul(
                    PS5[:, :], ONES[p:p + 1, 0:P], row(R, t),
                    start=True, stop=True).then_inc(TMM, 1))
                t_bc = sch.inc("TMM")
                sch.emit("vector", lambda v, t_bc=t_bc: v.wait_ge(TMM, t_bc))
                sch.emit("vector", lambda v, t=tt: v.tensor_copy(bc(t), PS5[:, :]).then_inc(VEV, 1))
                v_bc = sch.inc("VEV")
                if tt < NT - 1:
                    sch.emit("tensor", lambda t_, v_bc=v_bc: t_.wait_ge(VEV, v_bc))

        sch.emit("tensor", lambda t_: t_.wait_ge(LDG, LD_WIN))
        for half in range(2):
            dst = PS4 if half == 0 else PS3
            for dch in range(ND):
                last = half == 1 and dch == ND - 1
                sch.emit("tensor", lambda t_, d=dch, h=half, dst=dst, last=last: (
                    t_.matmul(dst[0:1, :], ONES[:, 0:1],
                              WIN[:, d * D + h * TS: d * D + (h + 1) * TS],
                              start=(d == 0), stop=(d == ND - 1)).then_inc(TMM, 1)
                    if last else
                    t_.matmul(dst[0:1, :], ONES[:, 0:1],
                              WIN[:, d * D + h * TS: d * D + (h + 1) * TS],
                              start=(d == 0), stop=(d == ND - 1))))
        T_W1 = sch.inc("TMM")
        sch.emit("vector", lambda v: v.wait_ge(TMM, T_W1))
        sch.emit("vector", lambda v: v.tensor_scalar(
            W1B[0:1, 0:TS], PS4[0:1, :], -1.0, None, op0=ALU.mult))
        sch.emit("vector", lambda v: v.tensor_scalar(
            W1B[0:1, TS:D], PS3[0:1, :], -1.0, None, op0=ALU.mult))
        sch.emit("vector", lambda v: v.tensor_copy(W1B[32:33, :], W1B[0:1, :]).then_inc(VEV, 1))
        V_W1 = sch.inc("VEV")

        stat_banks = [PS5, PS6, PS7, PS2]
        for tt in range(NT):
            sch.emit("tensor", lambda t_, t=tt: t_.wait_ge(LX[t], 16))
            for dch in range(ND):
                last = dch == ND - 1
                sch.emit("tensor", lambda t_, d=dch, t=tt, bk=stat_banks[tt], last=last: (
                    t_.matmul(bk[0:1, :], ONES[:, 0:1], xbf(d, t),
                              start=(d == 0), stop=(d == ND - 1)).then_inc(TMM, 1)
                    if last else
                    t_.matmul(bk[0:1, :], ONES[:, 0:1], xbf(d, t),
                              start=(d == 0), stop=(d == ND - 1))))
            sch.inc("TMM")
        T_SX = sch.val("TMM")
        sch.emit("vector", lambda v: v.wait_ge(TMM, T_SX))
        for tt in range(NT):
            sch.emit("vector", lambda v, t=tt: v.tensor_copy(
                row(RS1, t), stat_banks[t][0:1, :]))
        sch.emit("vector", lambda v: v.tensor_copy(TM8[:, 1:2], ZCOL[:, :]).then_inc(VEV, 1))
        V_SXEV = sch.inc("VEV")
        sch.emit("tensor", lambda t_: t_.wait_ge(VEV, V_SXEV))
        for tt in range(NT):
            for dch in range(ND):
                slot = XSQ[:, (dch % 2) * TS:(dch % 2 + 1) * TS]
                if tt * ND + dch >= 2:
                    need = sch.val("TMM") - 1
                    sch.emit("vector", lambda v, need=need: v.wait_ge(TMM, need))
                sch.emit("vector", lambda v, sl=slot, d=dch, t=tt: v.tensor_tensor(
                    sl, xbf(d, t), xbf(d, t), op=ALU.mult).then_inc(VEV, 1))
                vney = sch.inc("VEV")
                sch.emit("tensor", lambda t_, vney=vney: t_.wait_ge(VEV, vney))
                sch.emit("tensor", lambda t_, sl=slot, d=dch, t=tt, bk=stat_banks[tt]: t_.matmul(
                    bk[0:1, :], ONES[:, 0:1], sl,
                    start=(d == 0), stop=(d == ND - 1)).then_inc(TMM, 1))
                sch.inc("TMM")
            t_s2 = sch.val("TMM")
            sch.emit("vector", lambda v, t_s2=t_s2: v.wait_ge(TMM, t_s2))
            sch.emit("vector", lambda v, t=tt: v.tensor_copy(
                row(RS2, t), stat_banks[t][0:1, :]).then_inc(VEV, 1))
            sch.inc("VEV")
        s_rows1 = ln_rows(with_sigma=True, inv_bias=0.0)
        sch.emit("tensor", lambda t_: t_.wait_ge(SCL, s_rows1))
        bcast_inv(BINV)

        sch.emit("tensor", lambda t_: t_.wait_ge(VEV, V_W1))
        sch.emit("vector", lambda v: v.wait_ge(SCL, S_DECAY))
        bank_free = [0, 0]
        for ech in range(NE):
            for tt in range(NT):
                i = ech * NT + tt
                bk = PS[i % 2]
                if bank_free[i % 2]:
                    sch.emit("tensor", lambda t_, w=bank_free[i % 2]: t_.wait_ge(VEV, w))
                for dch in range(ND):
                    sch.emit("tensor", lambda t_, d=dch, e=ech, t=tt, bk=bk: t_.matmul(
                        bk[:, :], win(d, e), xbf(d, t), start=(d == 0), stop=False))
                p = rowp(tt)
                sch.emit("tensor", lambda t_, p=p, e=ech, t=tt, bk=bk: t_.matmul(
                    bk[:, :], W1B[p:p + 1, e * P:(e + 1) * P], row(BMU, t),
                    start=False, stop=False))
                sch.emit("tensor", lambda t_, p=p, e=ech, t=tt, bk=bk: t_.matmul(
                    bk[:, :], RBIN[p:p + 1, e * P:(e + 1) * P], row(BSIG, t),
                    start=False, stop=True).then_inc(TMM, 1))
                t_grp = sch.inc("TMM")
                sch.emit("vector", lambda v, t_grp=t_grp: v.wait_ge(TMM, t_grp))
                sch.emit("vector", lambda v, e=ech, t=tt, bk=bk: v.scalar_tensor_tensor(
                    usc(e, t), bk[:, :], vcol("b_p", e), bc(t),
                    op0=ALU.mult, op1=ALU.mult).then_inc(VEV, 1))
                bank_free[i % 2] = sch.inc("VEV")
            sch.emit("vector", lambda v, e=ech: v.tensor_copy(DEC1[:, :], APP[:, e:e + 1]))
            sch.emit("vector", lambda v, e=ech: v.tensor_tensor_scan(
                SCNBUF[:, :], DEC1[:, 0:1].broadcast_to((P, TL)), uscfull(e),
                ZCOL[:, 0:1], op0=ALU.mult, op1=ALU.add).then_inc(VEV, 1))
            vsc = sch.inc("VEV")
            sch.emit("vector", lambda v, vsc=vsc: v.wait_ge(VEV, vsc))
            sch.emit("vector", lambda v, e=ech: v.tensor_copy(
                SEND[:, e:e + 1], SCNBUF[:, TL - 1:TL]))
        T_A2 = sch.val("TMM")
        sch.emit("gpsimd", lambda g: g.wait_ge(TMM, T_A2))
        sch.emit("gpsimd", lambda g: g.dma_start(out=WIN[:, :], in_=wont[:, :]).then_inc(LDG, 16))
        LD_WOUT = sch.inc("LDG", 16)
        sch.emit("vector", lambda v: v.tensor_copy(TM8[:, 0:1], ZCOL[:, :]).then_inc(VEV, 1))
        V_SCAN1 = sch.inc("VEV")

        sch.emit("gpsimd", lambda g: g.wait_ge(VEV, V_SCAN1))
        sch.emit("gpsimd", lambda g: g.dma_start(out=cc_in[:], in_=SEND[:, :]).then_inc(LDG, 16))
        LD_CCIN = sch.inc("LDG", 16)
        sch.emit("gpsimd", lambda g: g.wait_ge(LDG, LD_CCIN))
        sch.emit("gpsimd", lambda g: g.collective_compute(
            "AllGather", ALU.bypass, replica_groups=[list(range(8))],
            ins=[cc_in.ap().opt()], outs=[cc_out.ap().opt()]).then_inc(CCS, 1))
        sch.inc("CCS")
        sch.emit("gpsimd", lambda g: g.wait_ge(CCS, 1))
        sch.emit("gpsimd", lambda g: g.dma_start(out=GBF[0:8, :], in_=cc_out[:, :]).then_inc(LDG, 16))
        LD_GBF = sch.inc("LDG", 16)

        sch.emit("tensor", lambda t_: t_.wait_ge(LDG, LD_GBF))
        for ch in range(NE):
            last = ch == NE - 1
            sch.emit("tensor", lambda t_, c=ch, last=last: (
                t_.matmul(PS5[:, c:c + 1], GBF[0:8, c::NE], SELB[:, :],
                          start=True, stop=True).then_inc(TMM, 1)
                if last else
                t_.matmul(PS5[:, c:c + 1], GBF[0:8, c::NE], SELB[:, :],
                          start=True, stop=True)))
        T_SEL = sch.inc("TMM")
        sch.emit("scalar", lambda s: s.wait_ge(TMM, T_SEL))
        sch.emit("scalar", lambda s: s.copy(PT[:, :], PS5[:, 0:NE]).then_inc(SCL, 1))
        S_PT = sch.inc("SCL")

        sch.emit("vector", lambda v: v.wait_ge(SCL, S_PT))
        sch.emit("vector", lambda v: v.tensor_copy(SST[:, :], PT[:, :]).then_inc(VEV, 1))
        v_sst = sch.inc("VEV")
        sch.emit("vector", lambda v, v_sst=v_sst: v.wait_ge(VEV, v_sst))
        sch.emit("tensor", lambda t_: t_.wait_ge(LDG, LD_WOUT))
        y_done = {}
        for tt in range(NT):
            for ech in range(NE):
                sch.emit("vector", lambda v, e=ech: v.tensor_copy(DEC1[:, :], APP[:, e:e + 1]))
                sch.emit("vector", lambda v, e=ech, t=tt: v.tensor_tensor_scan(
                    SCNBUF[:, t * TS:(t + 1) * TS],
                    DEC1[:, 0:1].broadcast_to((P, TS)), usc(e, t),
                    SST[:, e:e + 1], op0=ALU.mult, op1=ALU.add).then_inc(VEV, 1))
                vsc2 = sch.inc("VEV")
                sch.emit("vector", lambda v, vsc2=vsc2: v.wait_ge(VEV, vsc2))
                sch.emit("vector", lambda v, e=ech, t=tt: v.tensor_copy(
                    SST[:, e:e + 1], SCNBUF[:, (t + 1) * TS - 1:(t + 1) * TS]))
                sch.emit("vector", lambda v, e=ech, t=tt: v.tensor_scalar(
                    bf1(e, t), SCNBUF[:, t * TS:(t + 1) * TS],
                    vcol("c_p", e), None, op0=ALU.mult).then_inc(VEV, 1))
                sch.inc("VEV")
            y_done[tt] = sch.val("VEV")

        bank_free = [None, None]
        rpair = [(PS2, PS3), (PS4, PS5)]
        rb_free = {}
        xn2_done = {}
        stats_bank_free = 0
        bc_free = 0

        def row_ops_tt(tt, inv_bias, with_sigma):
            p = rowp(tt)

            def rs(R):
                return R[p:p + 1, (tt % 2) * TS:(tt % 2) * TS + TS]

            sch.emit("vector", lambda v: v.tensor_scalar(
                rs(RMU), rs(RS1), 1.0 / D, None, op0=ALU.mult))
            sch.emit("vector", lambda v: v.tensor_tensor(
                rs(RS1), rs(RMU), rs(RMU), op=ALU.mult))
            sch.emit("vector", lambda v: v.tensor_scalar(
                rs(RS2), rs(RS2), 1.0 / D, None, op0=ALU.mult))
            sch.emit("vector", lambda v: v.tensor_tensor(
                rs(RS2), rs(RS2), rs(RS1), op=ALU.subtract).then_inc(VEV, 1))
            v_rows = sch.inc("VEV")
            sch.emit("scalar", lambda s, v_rows=v_rows: s.wait_ge(VEV, v_rows))
            sch.emit("scalar", lambda s: s.activation(rs(RS2), rs(RS2), ACT.Ln, bias=ccol("eps", p, 1)))
            if inv_bias == 0.0:
                sch.emit("scalar", lambda s: s.activation(
                    rs(BINV), rs(RS2), ACT.Exp, scale=ccol("mhalf", p, 1)))
            else:
                sch.emit("scalar", lambda s: s.activation(
                    rs(BINV), rs(RS2), ACT.Exp, scale=ccol("mhalf", p, 1), bias=ccol("mln127", p, 1)))
            if with_sigma:
                sch.emit("scalar", lambda s: s.activation(
                    rs(RS1), rs(RS2), ACT.Exp, scale=ccol("mhalf", p, 1)).then_inc(SCL, 1))
                s_inv = sch.inc("SCL")
                sch.emit("vector", lambda v, s_inv=s_inv: v.wait_ge(SCL, s_inv))
                sch.emit("vector", lambda v: v.tensor_tensor(
                    rs(RMU), rs(RMU), rs(RS1), op=ALU.mult).then_inc(VEV, 1))
                v_m2 = sch.inc("VEV")
                sch.emit("scalar", lambda s, v_m2=v_m2: s.wait_ge(VEV, v_m2))
            sch.emit("scalar", lambda s: s.activation(
                rs(BMU), rs(RMU), ACT.Copy).then_inc(SCL, 1))
            return sch.inc("SCL")

        def stats_tt(tt, src_tile_fn, pre_copy, gate=None):
            nonlocal stats_bank_free
            if stats_bank_free:
                sch.emit("tensor", lambda t_, w=stats_bank_free: t_.wait_ge(VEV, w))
            for ch in range(ND):
                if gate is not None:
                    sch.emit("vector", lambda v, w=gate[2][ch], sem=gate[1]: v.wait_ge(sem, w))
                uslot = SG[:, (ch % 2) * TS:(ch % 2 + 1) * TS]
                qslot = XSQ[:, (ch % 2) * TS:(ch % 2 + 1) * TS]
                if ch >= 2:
                    need = sch.val("TMM") - 1
                    sch.emit("vector", lambda v, need=need: v.wait_ge(TMM, need))
                if pre_copy:
                    sch.emit("vector", lambda v, u=uslot, c=ch, t=tt: v.tensor_copy(
                        u, usc(c, t)))
                    sch.emit("vector", lambda v, u=uslot, q=qslot: v.tensor_tensor(
                        q, u, u, op=ALU.mult).then_inc(VEV, 1))
                    src = uslot
                else:
                    sch.emit("vector", lambda v, q=qslot, c=ch, t=tt: v.tensor_tensor(
                        q, src_tile_fn(c, t), src_tile_fn(c, t), op=ALU.mult).then_inc(VEV, 1))
                    src = None
                vney = sch.inc("VEV")
                sch.emit("tensor", lambda t_, vney=vney: t_.wait_ge(VEV, vney))
                if pre_copy:
                    sch.emit("tensor", lambda t_, u=uslot, c=ch: t_.matmul(
                        PS6[0:1, :], ONES[:, 0:1], u,
                        start=(c == 0), stop=(c == ND - 1)))
                else:
                    sch.emit("tensor", lambda t_, c=ch, t=tt: t_.matmul(
                        PS6[0:1, :], ONES[:, 0:1], src_tile_fn(c, t),
                        start=(c == 0), stop=(c == ND - 1)))
                sch.emit("tensor", lambda t_, q=qslot, c=ch: t_.matmul(
                    PS7[0:1, :], ONES[:, 0:1], q,
                    start=(c == 0), stop=(c == ND - 1)).then_inc(TMM, 1))
                sch.inc("TMM")
            t_stats = sch.val("TMM")
            p = rowp(tt)
            fs = (tt % 2) * TS
            sch.emit("vector", lambda v, t_stats=t_stats: v.wait_ge(TMM, t_stats))
            sch.emit("vector", lambda v, p=p, fs=fs: v.tensor_copy(
                RS1[p:p + 1, fs:fs + TS], PS6[0:1, :]))
            sch.emit("vector", lambda v, p=p, fs=fs: v.tensor_copy(
                RS2[p:p + 1, fs:fs + TS], PS7[0:1, :]).then_inc(VEV, 1))
            stats_bank_free = sch.inc("VEV")

        def emit_wout_groups(tt):
            for fch in range(NE):
                i = tt * NE + fch
                bk = PS[i % 2]
                if fch == 0:
                    sch.emit("tensor", lambda t_, w=y_done[tt]: t_.wait_ge(VEV, w))
                if bank_free[i % 2]:
                    sch.emit("tensor", lambda t_, w=bank_free[i % 2][1]: t_.wait_ge(SCL, w))
                for ech in range(NE):
                    sch.emit("tensor", lambda t_, e=ech, f=fch, t=tt, bk=bk: t_.matmul(
                        bk[:, :], win(e, f), bf1(e, t), start=(e == 0), stop=False))
                sch.emit("tensor", lambda t_, f=fch, bk=bk: t_.matmul(
                    bk[:, :], RBOUT[0:1, f * P:(f + 1) * P], ONES[0:1, :],
                    start=False, stop=True).then_inc(TMM, 1))
                t_grp = sch.inc("TMM")
                sch.emit("scalar", lambda s, t_grp=t_grp: s.wait_ge(TMM, t_grp))
                sch.emit("scalar", lambda s, f=fch, t=tt, bk=bk: s.copy(
                    usc(f, t), bk[:, :]).then_inc(SCL, 1))
                bank_free[i % 2] = ("SCL", sch.inc("SCL"))
            wout_T[tt] = sch.val("TMM")

        def post_stage(tt):
            nonlocal stats_bank_free, bc_free
            p = rowp(tt)
            obf_ch = []
            for fch in range(NE):
                sch.emit("scalar", lambda s, f=fch, t=tt: s.activation(
                    bf1(f, t), usc(f, t), ACT.Copy).then_inc(SCL, 1))
                obf_ch.append(sch.inc("SCL"))
            stats_tt(tt, bf1, pre_copy=False, gate=("SCL", SCL, obf_ch))
            s_rows2 = row_ops_tt(tt, inv_bias=0.0, with_sigma=True)
            sch.emit("tensor", lambda t_, s=s_rows2: t_.wait_ge(SCL, s))
            for fch in range(NE):
                ra, rb = rpair[fch % 2]
                if rb_free.get(fch % 2):
                    sch.emit("tensor", lambda t_, w=rb_free[fch % 2]: t_.wait_ge(VEV, w))
                sch.emit("tensor", lambda t_, p=p, f=fch, t=tt, ra=ra: t_.matmul(
                    ra[:, :], RGAM[p:p + 1, f * P:(f + 1) * P],
                    row(BINV, t), start=True, stop=True))
                sch.emit("tensor", lambda t_, p=p, f=fch, t=tt, rb=rb: t_.matmul(
                    rb[:, :], RGAM[p:p + 2, f * P:(f + 1) * P],
                    BMU[p:p + 2, (t % 2) * TS:(t % 2) * TS + TS],
                    start=True, stop=True).then_inc(TMM, 1))
                t_r = sch.inc("TMM")
                sch.emit("vector", lambda v, t_r=t_r: v.wait_ge(TMM, t_r))
                sch.emit("vector", lambda v, f=fch, t=tt, ra=ra: v.tensor_tensor(
                    TMPV[:, :], usc(f, t), ra[:, :], op=ALU.mult))
                sch.emit("vector", lambda v, f=fch, t=tt, rb=rb: v.scalar_tensor_tensor(
                    usc(f, t), TMPV[:, :], 1.0, rb[:, :],
                    op0=ALU.mult, op1=ALU.subtract).then_inc(VEV, 1))
                rb_free[fch % 2] = sch.inc("VEV")
                v_t2 = rb_free[fch % 2]
                sch.emit("gpsimd", lambda g, v_t2=v_t2: g.wait_ge(VEV, v_t2))
                sch.emit("gpsimd", lambda g, f=fch, t=tt: g.dma_start(
                    out=usc(f, t), in_=xt[:, f * TL + t * TS: f * TL + (t + 1) * TS],
                    accum_op=ALU.add).then_inc(XAC, 16))
                sch.inc("XAC", 16)
            u_ready = sch.val("XAC")
            sch.emit("vector", lambda v, u_ready=u_ready: v.wait_ge(XAC, u_ready))
            stats_tt(tt, None, pre_copy=True)
            s_rows3 = row_ops_tt(tt, inv_bias=-LN127, with_sigma=False)
            sch.emit("tensor", lambda t_, s=s_rows3: t_.wait_ge(SCL, s))
            if bc_free:
                sch.emit("tensor", lambda t_, w=bc_free: t_.wait_ge(VEV, w))
            sch.emit("tensor", lambda t_, p=p, t=tt: t_.matmul(
                PS4[:, :], ONES[p:p + 1, 0:P], row(BMU, t), start=True, stop=True))
            sch.emit("tensor", lambda t_, p=p, t=tt: t_.matmul(
                PS5[:, :], ONES[p:p + 1, 0:P], row(BINV, t), start=True, stop=True).then_inc(TMM, 1))
            t_bc = sch.inc("TMM")
            sch.emit("vector", lambda v, t_bc=t_bc: v.wait_ge(TMM, t_bc))
            for dch in range(ND):
                sch.emit("vector", lambda v, d=dch, t=tt: v.tensor_tensor(
                    TMPV[:, :], usc(d, t), PS4[:, :], op=ALU.subtract))
                sch.emit("vector", lambda v, d=dch, t=tt: v.tensor_tensor(
                    bf1(d, t), TMPV[:, :], PS5[:, :], op=ALU.mult).then_inc(VEV, 1))
                sch.inc("VEV")
            bc_free = sch.val("VEV")
            xn2_done[tt] = bc_free
            rb_free[0] = rb_free[1] = bc_free

        wout_T = {}
        for tt in range(NT):
            emit_wout_groups(tt)
            if tt >= 2:
                post_stage(tt - 2)
        post_stage(NT - 2)
        post_stage(NT - 1)
        T_WOUT_END = wout_T[NT - 1]
        T_B1 = sch.val("TMM")
        S_B1_EV = sch.val("SCL")
        V_C = sch.val("VEV")

        sch.emit("gpsimd", lambda g: g.wait_ge(TMM, T_WOUT_END))
        stg_free = [0, 0]
        wb_val = {}
        wdg = {}
        for half in range(2):
            gu_bank_free = {0: 0, 1: 0}
            gu_wbuf_free = {}
            NGU = 3
            for hch in range(NH):
                slot = hch % NGU
                if hch >= NGU:
                    sch.emit("gpsimd", lambda g, w=gu_wbuf_free[hch - NGU]: g.wait_ge(VEV, w))
                sch.emit("gpsimd", lambda g, h=hch, sl=slot: g.dma_start(
                    out=wgb(sl), in_=wg[h, :, :]).then_inc(WGS[sl], 16))
                wg_v = sch.inc("WGS%d" % slot, 16)
                sch.emit("gpsimd", lambda g, h=hch, sl=slot: g.dma_start(
                    out=wub(sl), in_=wu[h, :, :]).then_inc(WUS[sl], 16))
                wu_v = sch.inc("WUS%d" % slot, 16)
                sch.emit("tensor", lambda t_, v=wg_v, sl=slot: t_.wait_ge(WGS[sl], v))
                sch.emit("tensor", lambda t_, v=wu_v, sl=slot: t_.wait_ge(WUS[sl], v))
                if half == 0:
                    csl = hch % 4
                    if hch == 0:
                        sch.emit("gpsimd", lambda g: g.wait_ge(TMM, T_WOUT_END))
                    if hch >= 4:
                        w = wb_val[hch - 4]
                        sch.emit("gpsimd", lambda g, w=w, csl=csl: g.wait_ge(WB[csl], w))
                    sch.emit("gpsimd", lambda g, h=hch, csl=csl: g.dma_start(
                        out=wdb(csl), in_=wd[h, :, :]).then_inc(WDS[csl], 16))
                    cl_v = sch.inc("WDS%d" % csl, 16)
                    sch.emit("sync", lambda sy, v=cl_v, csl=csl: sy.wait_ge(WDS[csl], v))
                    sch.emit("sync", lambda sy, h=hch, csl=csl: sy.dma_start(
                        out=wdbf[h, :, :], in_=wdb(csl)).then_inc(WB[csl], 16))
                    wb_val[hch] = sch.inc("WB%d" % csl, 16)
                if hch == 0:
                    w = xn2_done[half * 2 + 1]
                    sch.emit("tensor", lambda t_, w=w: t_.wait_ge(VEV, w))
                    sch.emit("tensor", lambda t_: t_.wait_ge(SCL, S_B1_EV))
                for tile in range(2):
                    gtt = half * 2 + tile
                    gbank, ubank = PS[tile], PS[2 + tile]
                    if gu_bank_free[tile]:
                        sch.emit("tensor", lambda t_, w=gu_bank_free[tile]: t_.wait_ge(VEV, w))
                    for dch in range(ND):
                        sch.emit("tensor", lambda t_, d=dch, sl=slot, g=gtt, bk=gbank: t_.matmul(
                            bk[:, :], wgb(sl)[:, d * P:(d + 1) * P], bf1(d, g),
                            start=(d == 0), stop=(d == ND - 1)))
                    for dch in range(ND):
                        last = dch == ND - 1
                        sch.emit("tensor", lambda t_, d=dch, sl=slot, g=gtt, bk=ubank, last=last: (
                            t_.matmul(bk[:, :], wub(sl)[:, d * P:(d + 1) * P], bf1(d, g),
                                      start=(d == 0), stop=last).then_inc(TMM, 1)
                            if last else
                            t_.matmul(bk[:, :], wub(sl)[:, d * P:(d + 1) * P], bf1(d, g),
                                      start=(d == 0), stop=last)))
                    t_gu = sch.inc("TMM")
                    sgs = SG[:, tile * TS:(tile + 1) * TS]
                    sch.emit("scalar", lambda s, t_gu=t_gu: s.wait_ge(TMM, t_gu))
                    sch.emit("scalar", lambda s, sgs=sgs, bk=gbank: s.activation(
                        sgs, bk[:, :], ACT.Silu).then_inc(SCL, 1))
                    s_silu = sch.inc("SCL")
                    sch.emit("vector", lambda v, s_silu=s_silu: v.wait_ge(SCL, s_silu))
                    sch.emit("vector", lambda v, h=hch, tile=tile, sgs=sgs, bk=ubank: v.scalar_tensor_tensor(
                        HH[:, h * 1024 + tile * TS: h * 1024 + (tile + 1) * TS],
                        bk[:, :], 1.0 / 127.0, sgs,
                        op0=ALU.mult, op1=ALU.mult).then_inc(VEV, 1))
                    gu_bank_free[tile] = sch.inc("VEV")
                gu_wbuf_free[hch] = sch.val("VEV")
            for tile in range(2):
                gtt = half * 2 + tile
                for hgrp in range(6):
                    par = hgrp % 2
                    if hgrp >= 2:
                        key = (half, tile, hgrp - 2)
                    else:
                        ph, pt = (half, tile - 1) if tile > 0 else (half - 1, 1)
                        key = (ph, pt, hgrp + 4)
                    if key in wdg:
                        sch.emit("gpsimd", lambda g, w=wdg[key]: g.wait_ge(TMM, w))
                    h0_ = hgrp * 4
                    if par == 0:
                        if half == 0 and tile == 0 and hgrp == 0:
                            for j in range(4):
                                sch.emit("gpsimd", lambda g, w=wb_val[20 + j], sl=j: g.wait_ge(WB[sl], w))
                        sch.emit("gpsimd", lambda g, h=h0_: g.dma_start(
                            out=WIN[:, 2048:4096],
                            in_=wdbf[h:h + 2, :, :].rearrange("k p c -> p k c")).then_inc(WDSX[0], 16))
                        wv0 = sch.inc("WDS0", 16)
                        sch.emit("gpsimd", lambda g, h=h0_: g.dma_start(
                            out=BCB[:, 0:2048],
                            in_=wdbf[h + 2:h + 4, :, :].rearrange("k p c -> p k c")).then_inc(WDSX[2], 16))
                        wv1 = sch.inc("WDS2", 16)
                        sch.emit("tensor", lambda t_, v=wv0: t_.wait_ge(WDSX[0], v))
                        sch.emit("tensor", lambda t_, v=wv1: t_.wait_ge(WDSX[2], v))
                    else:
                        sch.emit("gpsimd", lambda g, h=h0_: g.dma_start(
                            out=SCNB[:, 0:4096],
                            in_=wdbf[h:h + 4, :, :].rearrange("k p c -> p k c")).then_inc(WDSX[4], 16))
                        wv0 = sch.inc("WDX4", 16)
                        sch.emit("tensor", lambda t_, v=wv0: t_.wait_ge(WDSX[4], v))
                    if hgrp == 0:
                        w = sch.val("VEV")
                        sch.emit("tensor", lambda t_, w=w: t_.wait_ge(VEV, w))
                    for dch in range(ND):
                        for j in range(4):
                            h = hgrp * 4 + j
                            sl = par * 4 + j
                            last = dch == ND - 1 and j == 3
                            sch.emit("tensor", lambda t_, d=dch, sl=sl, h=h, tile=tile, last=last: (
                                t_.matmul(PS[d][:, :], wdb2(sl)[:, d * P:(d + 1) * P],
                                          HH[:, h * 1024 + tile * TS: h * 1024 + (tile + 1) * TS],
                                          start=(h == 0), stop=(h == NH - 1)).then_inc(TMM, 1)
                                if last else
                                t_.matmul(PS[d][:, :], wdb2(sl)[:, d * P:(d + 1) * P],
                                          HH[:, h * 1024 + tile * TS: h * 1024 + (tile + 1) * TS],
                                          start=(h == 0), stop=(h == NH - 1))))
                    wdg[(half, tile, hgrp)] = sch.inc("TMM")
                t_down = sch.val("TMM")
                for dch in range(ND):
                    sl = dch % 2
                    if stg_free[sl]:
                        sch.emit("vector", lambda v, w=stg_free[sl], sl=sl: v.wait_ge(SO[sl], w))
                    sch.emit("vector", lambda v, t_down=t_down: v.wait_ge(TMM, t_down))
                    sch.emit("vector", lambda v, d=dch, g=gtt, sl=sl: v.tensor_tensor(
                        stg(sl), PS[d][:, :], usc(d, g), op=ALU.add).then_inc(VEV, 1))
                    v_st = sch.inc("VEV")
                    sch.emit("sync", lambda s, v_st=v_st: s.wait_ge(VEV, v_st))
                    sch.emit("sync", lambda s, d=dch, g=gtt, sl=sl: s.dma_start(
                        out=yout[:, d * TL + g * TS: d * TL + (g + 1) * TS],
                        in_=stg(sl)).then_inc(SO[sl], 16))
                    stg_free[sl] = sch.inc("SO%d" % sl, 16)
                v_evac = sch.val("VEV")
                sch.emit("tensor", lambda t_, v_evac=v_evac: t_.wait_ge(VEV, v_evac))

        T0, T1 = sch.val("SO0"), sch.val("SO1")
        sch.emit("sync", lambda s: s.wait_ge(SO[0], T0))
        sch.emit("sync", lambda s: s.wait_ge(SO[1], T1))

        @block.gpsimd
        def _(g):
            for f in sch.q["gpsimd"]:
                f(g)

        @block.sync
        def _(s):
            for f in sch.q["sync"]:
                f(s)

        @block.scalar
        def _(s):
            for f in sch.q["scalar"]:
                f(s)

        @block.vector
        def _(v):
            for f in sch.q["vector"]:
                f(v)

        @block.tensor
        def _(t_):
            for f in sch.q["tensor"]:
                f(t_)

    return nc


def kernel(L0_attn_x, L0_attn_W_in, L0_attn_b_in, L0_attn_log_A, L0_attn_B_p,
           L0_attn_C_p, L0_attn_W_out, L0_attn_b_out, L0_attn_gamma,
           L0_attn_beta, L0_ffn_w_gate, L0_ffn_w_up, L0_ffn_w_down):
    x = np.asarray(L0_attn_x, dtype=np.float32)
    w_in = np.asarray(L0_attn_W_in, dtype=np.float32)
    w_out = np.asarray(L0_attn_W_out, dtype=np.float32)
    b_in = np.asarray(L0_attn_b_in, dtype=np.float32)
    b_out = np.asarray(L0_attn_b_out, dtype=np.float32)
    gamma = np.asarray(L0_attn_gamma, dtype=np.float32)
    beta = np.asarray(L0_attn_beta, dtype=np.float32)
    log_a = np.asarray(L0_attn_log_A, dtype=np.float32).reshape(-1)
    b_p = np.asarray(L0_attn_B_p, dtype=np.float32).reshape(-1)
    c_p = np.asarray(L0_attn_C_p, dtype=np.float32).reshape(-1)
    wgi = np.asarray(L0_ffn_w_gate, dtype=np.int32)
    wui = np.asarray(L0_ffn_w_up, dtype=np.int32)
    wdi = np.asarray(L0_ffn_w_down, dtype=np.int32)

    def lhsT_tiles(wT, kc, mc):
        return np.ascontiguousarray(
            wT.reshape(kc, P, mc, P).transpose(1, 0, 2, 3).reshape(P, kc * mc * P))

    wint_h = lhsT_tiles(w_in.T, ND, NE)
    wont_h = lhsT_tiles(w_out.T, NE, ND)
    wg_h = np.ascontiguousarray(
        wgi.T.reshape(ND, P, NH, P).transpose(2, 1, 0, 3).reshape(NH, P, ND * P))
    wu_h = np.ascontiguousarray(
        wui.T.reshape(ND, P, NH, P).transpose(2, 1, 0, 3).reshape(NH, P, ND * P))
    wd_h = np.ascontiguousarray(
        wdi.T.reshape(NH, P, ND, P).reshape(NH, P, ND * P))
    consts = np.repeat(np.array([[EPS, -0.5, 0.5, -LN127]], np.float32), P, axis=0)
    vpp_h = np.ascontiguousarray(np.concatenate(
        [v.reshape(8, P).T for v in (log_a, b_p, c_p, gamma, beta)] + [consts], axis=1))
    rows_h = np.ascontiguousarray(np.stack([b_in, b_out, gamma, beta,
                                            np.full(D, -1.0, np.float32)]))

    nc = build_nc()
    in_maps = []
    for k in range(8):
        b, half = k // 2, k % 2
        sl = slice(half * TL, (half + 1) * TL)
        xt_h = np.ascontiguousarray(
            x[b, sl, :].T.reshape(ND, P, TL).transpose(1, 0, 2).reshape(P, ND * TL))
        sel = np.zeros((8, 1), np.float32)
        if half == 1:
            sel[k - 1, 0] = 1.0
        in_maps.append({
            "xt": xt_h, "wint": wint_h, "wont": wont_h,
            "wg": wg_h, "wu": wu_h, "wd": wd_h,
            "vpp": vpp_h, "rows": rows_h, "selv": sel,
        })

    res = run_bass_kernel_spmd(nc, in_maps, core_ids=list(range(8)))

    out = np.empty((4, 4096, 1024), np.float32)
    for k in range(8):
        b, half = k // 2, k % 2
        sl = slice(half * TL, (half + 1) * TL)
        yo = np.asarray(res.results[k]["yout"])
        out[b, sl, :] = yo.reshape(P, ND, TL).transpose(2, 1, 0).reshape(TL, D)
    return out
